# version 01
# baseline (speedup 1.0000x reference)
import numpy as np
import ml_dtypes
import concourse.bass as bass
import concourse.mybir as mybir
from concourse.bass_utils import run_bass_kernel_spmd

F32 = mybir.dt.float32
BF16 = mybir.dt.bfloat16
F16 = mybir.dt.float16
I32 = mybir.dt.int32
AF = mybir.ActivationFunctionType
ALU = mybir.AluOpType
AX = mybir.AxisListType

T, D = 8192, 2048
NCORES = 8
TOK = T // NCORES
HALO = 512
NKV = TOK + HALO
H, HKV, DH = 16, 4, 128
FI = 4096
NT = TOK // 128
NKT = NKV // 128
NREL = 8
RSQ_DH = float(DH) ** -0.5
ISQ1 = 1.0 / float(np.sqrt(2048.0))
ISQ2 = 1.0 / 64.0

ENGS = ("sync", "scalar", "vector", "tensor", "gpsimd")
DBG = None


class Sem:
    __slots__ = ("h", "n")

    def __init__(self, h):
        self.h, self.n = h, 0


class B:

    def __init__(self):
        self.cmds = {e: [] for e in ENGS}

    def w(self, e, sem, val=None):
        v = sem.n if val is None else val
        if v > 0:
            self.cmds[e].append(lambda eng, s=sem, vv=v: eng.wait_ge(s.h, vv))

    def op(self, e, fn, sem=None, k=1):
        if sem is None:
            self.cmds[e].append(fn)
            return None
        sem.n += k
        v = sem.n
        self.cmds[e].append(lambda eng, f=fn, s=sem, kk=k: f(eng).then_inc(s.h, kk))
        return v

    def dma(self, e, out, in_, sem):
        return self.op(e, lambda eng, o=out, i=in_: eng.dma_start(out=o, in_=i), sem, 16)


def _bc_mid(ap, nh):
    return ap.unsqueeze(1).broadcast_to([128, nh, ap.shape[-1]])


def _bc_last(ap, n):
    return ap.unsqueeze(2).broadcast_to([128, ap.shape[-1], n])


def build_nc():
    nc = bass.Bass("TRN2", target_bir_lowering=False)
    dt = nc.dram_tensor
    xT = dt("xT", [D, NKV], F32, kind="ExternalInput").ap()
    xr = dt("xr", [TOK, D], F32, kind="ExternalInput").ap()
    wqT = dt("wqT", [D, D], F32, kind="ExternalInput").ap()
    wkT = dt("wkT", [D, 512], F32, kind="ExternalInput").ap()
    wvT = dt("wvT", [D, 512], F32, kind="ExternalInput").ap()
    woT = dt("woT", [D, D], F32, kind="ExternalInput").ap()
    wuT = dt("wuT", [D, FI], F32, kind="ExternalInput").ap()
    wdT = dt("wdT", [FI, D], F32, kind="ExternalInput").ap()
    cosqd = dt("cosq", [TOK, DH], F32, kind="ExternalInput").ap()
    sgqd = dt("sgq", [TOK, DH], F32, kind="ExternalInput").ap()
    coskd = dt("cosk", [NKV, DH], F32, kind="ExternalInput").ap()
    sgkd = dt("sgk", [NKV, DH], F32, kind="ExternalInput").ap()
    qnrd = dt("qnr", [128, DH], F32, kind="ExternalInput").ap()
    knrd = dt("knr", [128, DH], F32, kind="ExternalInput").ap()
    h128d = dt("h128", [128, 128], BF16, kind="ExternalInput").ap()
    identd = dt("ident", [128, 128], BF16, kind="ExternalInput").ap()
    onesfd = dt("onesf", [128, 2], F32, kind="ExternalInput").ap()
    onesrd = dt("onesr", [1, 128], F32, kind="ExternalInput").ap()
    onesbd = dt("onesb", [128, 1], BF16, kind="ExternalInput").ap()
    trid = dt("tri", [128, 256], BF16, kind="ExternalInput").ap()
    padcd = dt("padc", [1, 1024], F16, kind="ExternalInput").ap()
    y = dt("y", [TOK, D], F32, kind="ExternalOutput").ap()
    dbgb = dt("dbgb", [128, 32768], BF16, kind="ExternalOutput").ap() if DBG else None
    dbgf = dt("dbgf", [128, 8192], F32, kind="ExternalOutput").ap() if DBG else None

    ctx = []

    def sbuf(name, shape, dtp):
        cm = nc.sbuf_tensor(name, shape, dtp)
        hh = cm.__enter__()
        ctx.append(cm)
        return hh

    def psum(name, shape, dtp):
        cm = nc.psum_tensor(name, shape, dtp)
        hh = cm.__enter__()
        ctx.append(cm)
        return hh

    PS = [psum(f"ps{i}", [128, 512], F32) for i in range(6)]
    PB = psum("pbf", [128, 1024], BF16)

    C_h128 = sbuf("sb_h128", [128, 128], BF16)
    C_id = sbuf("sb_ident", [128, 128], BF16)
    C_1b = sbuf("sb_onesb", [128, 16], BF16)
    C_tri = sbuf("sb_tri", [128, 256], BF16)
    C_qbf = sbuf("qbf", [128, 256], BF16)
    R1 = sbuf("R1", [128, 28672], BF16)
    R2 = sbuf("R2", [128, 24576], BF16)
    R3 = sbuf("R3", [128, 4096], BF16)
    R4 = sbuf("R4", [128, 16384], BF16)
    Fc = sbuf("fc", [128, 544], F32)
    Fs0 = sbuf("fs0", [128, 2048], F32)
    Fs1 = sbuf("fs1", [128, 1024], F32)
    Ftb = sbuf("ftb", [128, 5120], F32)
    Frf = sbuf("frf", [128, 3072], F32)
    Fwst = sbuf("fwst", [128, 2048], F32)
    Fpc = sbuf("fpc", [128, 1024], F16)
    Gi = sbuf("gi", [128, 1024], I32)

    h128 = C_h128[:, :]
    ident = C_id[:, :]
    onesb = C_1b[:, 0:1]
    tri0 = C_tri[:, 0:128]
    tri4 = C_tri[:, 128:256]

    qf = R1[:, 0:16384].rearrange("p (h t) -> p h t", h=H)
    kf = R1[:, 16384:22528].rearrange("p (g t) -> p g t", g=HKV)
    vtok = R1[:, 22528:28672].rearrange("p (k c) -> p k c", k=NKT)
    aout = R1[:, 0:16384].rearrange("p (m c) -> p m c", m=NT)
    xq1f = R1[:, 0:16384].rearrange("p (k t) -> p k t", k=16)
    zA2 = R1[:, 0:4096]
    zB2 = R1[:, 4096:8192]
    h2f = R1[:, 16384:20480].rearrange("p (k t) -> p k t", k=32)
    r1f = R1[:, 16384:18432].rearrange("p (k t) -> p k t", k=16)
    xqt = R1[:, 20480:21504]
    wbF = R1[:, 21504:23552].rearrange("p (s c) -> p s c", s=8)
    zA1 = R1[:, 24576:26624]
    zB1 = R1[:, 26624:28672]
    x2blk7 = R1[:, 24576:28672]

    xfeat = R2[:, 0:24576].rearrange("p (k t) -> p k t", k=16)
    ctxf = R2[:, 0:16384].rearrange("p (h t) -> p h t", h=H)
    pslot = R2[:, 16384:20480].rearrange("p (r c) -> p r c", r=NREL)

    def X2(m):
        if m < 6:
            return R2[:, m * 4096:(m + 1) * 4096]
        if m == 6:
            return R3[:, 0:4096]
        return x2blk7

    wbP = R4[:, 0:2048].rearrange("p (s c) -> p s c", s=8)
    r1b = R4[:, 0:16384].rearrange("p (m c) -> p m c", m=NT)

    onesf = Fc[:, 0:1]
    onesr = Fc[:, 2:130]
    qn1 = Fc[:, 130:258]
    kn1 = Fc[:, 258:386]
    s1b = Fc[:, 386:394]
    A2c = Fc[:, 394:402]
    FAc = Fc[:, 402:410]
    mnc = Fc[:, 410:418]
    tin = Fc[:, 418:458]
    fq = Fc[:, 458:490]
    wup = Fc[:, 490:506]
    scr0 = Fs0[:, :]
    scr1 = Fs1[:, :]
    cq = Ftb[:, 0:1024].rearrange("p (m d) -> p m d", m=NT)
    sgq = Ftb[:, 1024:2048].rearrange("p (m d) -> p m d", m=NT)
    ck = Ftb[:, 2048:3584].rearrange("p (m d) -> p m d", m=NKT)
    sgk = Ftb[:, 3584:5120].rearrange("p (m d) -> p m d", m=NKT)
    xrb = Ftb[:, 0:4096].rearrange("p (s c) -> p s c", s=2)
    gf = Ftb[:, 0:1024]
    dd = Ftb[:, 1024:2048]
    outf = Ftb[:, 2048:4096]
    wdp = Fc[:, 512:544]
    xstage = Frf[:, 0:3072].rearrange("p (s c) -> p s c", s=2)
    kns = Frf[:, 0:3072].rearrange("p (m c) -> p m c", m=12)
    lrow = Frf[:, 0:512]
    lsb = Frf[:, 512:1024]
    dwn = Frf[:, 1024:3072].rearrange("p (s c) -> p s c", s=8)
    wst = Fwst[:, :].rearrange("p (g c) -> p g c", g=2)

    b = B()
    semc = []
    sems = []
    for name in ("L", "G", "T", "V", "A", "S"):
        cm = nc.semaphore("s" + name)
        hh = cm.__enter__()
        semc.append(cm)
        sems.append(Sem(hh))
    sL, sG, sT, sV, sA, sS = sems

    def tap(name, region_ap, nwords):
        if DBG != name:
            return
        for s_ in (sT, sV, sA):
            b.w("gpsimd", s_)
        tgt = dbgb if region_ap.dtype == BF16 else dbgf
        b.dma("gpsimd", tgt[:, 0:nwords], region_ap, sS)
        for e_ in ("tensor", "vector", "scalar", "sync"):
            b.w(e_, sS)

    def V(fn, inc=True):
        return b.op("vector", fn, sV if inc else None)

    def SC(fn, inc=True):
        return b.op("scalar", fn, sA if inc else None)

    def TE(fn, inc=True):
        return b.op("tensor", fn, sT if inc else None)

    def VF():
        V(lambda e: e.tensor_copy(tin[:, 38:40], onesr[:, 0:2]), inc=False)

    def FV():
        return V(lambda e: e.tensor_copy(tin[:, 38:40], onesr[:, 0:2]))

    def FA():
        return SC(lambda e: e.copy(tin[:, 36:38], onesr[:, 0:2]))

    for dst, src in ((h128, h128d), (ident, identd), (onesb, onesbd), (C_tri[:, :], trid),
                     (Fc[:, 0:2], onesfd), (onesr[0:1, :], onesrd), (qn1, qnrd), (kn1, knrd),
                     (cq, cosqd.rearrange("(m p) d -> p m d", p=128)),
                     (sgq, sgqd.rearrange("(m p) d -> p m d", p=128)),
                     (ck, coskd.rearrange("(m p) d -> p m d", p=128)),
                     (sgk, sgkd.rearrange("(m p) d -> p m d", p=128)),
                     (Fpc[0:1, :], padcd)):
        b.dma("sync", dst, src, sL)
    b.w("vector", sL)
    V(lambda e: e.tensor_scalar_add(qn1, qn1, 1.0))
    V(lambda e: e.tensor_scalar_add(kn1, kn1, 1.0))

    xc_cast = [0, 0]
    for kc in range(16):
        sl = kc % 2
        if kc >= 2:
            b.w("sync", sV, xc_cast[sl])
        ld = b.dma("sync", xstage[:, sl, :], xT[kc * 128:(kc + 1) * 128, :], sL)
        b.w("vector", sL, ld)
        xc_cast[sl] = V(lambda e, kc=kc, sl=sl: e.tensor_copy(xfeat[:, kc, :], xstage[:, sl, :]))
    FV()
    xfeat_done = sV.n

    for kc in range(16):
        b.dma("sync", scr0, wuT[kc * 128:(kc + 1) * 128, 0:2048], sL)
        b.dma("sync", scr1, wuT[kc * 128:(kc + 1) * 128, 2048:3072], sL)
        ld = b.dma("sync", Fwst[:, 0:1024], wuT[kc * 128:(kc + 1) * 128, 3072:4096], sL)
        b.w("vector", sL, ld)
        V(lambda e, kc=kc: e.tensor_reduce(wup[:, kc:kc + 1], scr0, AX.X, ALU.add, apply_absolute_value=True))
        V(lambda e: e.tensor_reduce(tin[:, 0:1], scr1, AX.X, ALU.add, apply_absolute_value=True))
        V(lambda e: e.tensor_reduce(tin[:, 1:2], Fwst[:, 0:1024], AX.X, ALU.add, apply_absolute_value=True))
        VF()
        V(lambda e: e.tensor_add(tin[:, 0:1], tin[:, 0:1], tin[:, 1:2]))
        VF()
        V(lambda e, kc=kc: e.tensor_add(wup[:, kc:kc + 1], wup[:, kc:kc + 1], tin[:, 0:1]))
    for kc in range(32):
        ld = b.dma("sync", scr0, wdT[kc * 128:(kc + 1) * 128, 0:2048], sL)
        b.w("vector", sL, ld)
        V(lambda e, kc=kc: e.tensor_reduce(wdp[:, kc:kc + 1], scr0, AX.X, ALU.add, apply_absolute_value=True))
    FV()
    b.w("tensor", sV)
    TE(lambda e: e.matmul(PS[5][0:1, 0:16], onesf, wup, start=True, stop=True))
    TE(lambda e: e.matmul(PS[5][0:1, 16:48], onesf, wdp, start=True, stop=True))
    b.w("vector", sT)
    V(lambda e: e.tensor_reduce(tin[0:1, 2:3], PS[5][0:1, 0:16], AX.X, ALU.add))
    V(lambda e: e.tensor_reduce(tin[0:1, 3:4], PS[5][0:1, 16:48], AX.X, ALU.add))
    VF()
    V(lambda e: e.tensor_scalar(tin[0:1, 4:5], tin[0:1, 2:3], 1.0 / (4096.0 * 2048.0), 1e-5, ALU.mult, ALU.max))
    V(lambda e: e.tensor_scalar(tin[0:1, 5:6], tin[0:1, 3:4], 1.0 / (4096.0 * 2048.0), 1e-5, ALU.mult, ALU.max))
    sw1 = FV()
    b.w("tensor", sV, sw1)
    TE(lambda e: e.matmul(PS[5][:, 64:66], onesr[0:1, :], tin[0:1, 4:6], start=True, stop=True))
    b.w("vector", sT)
    V(lambda e: e.tensor_copy(s1b[:, 0:1], PS[5][:, 64:65]))
    V(lambda e: e.tensor_copy(s1b[:, 3:4], PS[5][:, 65:66]))
    VF()
    V(lambda e: e.tensor_scalar_mul(s1b[:, 1:2], s1b[:, 0:1], 0.5))
    V(lambda e: e.tensor_scalar_mul(s1b[:, 2:3], s1b[:, 0:1], -0.5))
    V(lambda e: e.tensor_scalar_mul(s1b[:, 4:5], s1b[:, 3:4], 0.5))
    V(lambda e: e.tensor_scalar_mul(s1b[:, 5:6], s1b[:, 3:4], -0.5))

    wgrp_free = [0, 0]
    wbuf_free = {}

    def wb_key(wb, slot):
        return (id(wb.tensor), slot)

    def wgroup(dram, col0, width, g0, wb, which):
        g = (g0 // 4) % 2
        b.w("sync", sV, wgrp_free[g] if wgrp_free[g] else None)
        lds = 0
        for j in range(4):
            kc = g0 + j
            lds = b.dma("sync", wst[:, g, j * width:(j + 1) * width],
                        dram[kc * 128:(kc + 1) * 128, col0:col0 + width], sL)
        b.w("vector", sL, lds)
        for j in range(4):
            fv = wbuf_free.get(wb_key(wb, (g0 + j) % 8), 0)
            if fv:
                b.w("vector", sT, fv)
        n = 4 * width
        dst = wb[:, g0 % 8:g0 % 8 + 4, 0:width].rearrange("p s c -> p (s c)")
        src = wst[:, g, 0:n]
        if which is None:
            V(lambda e, d=dst, s=src: e.tensor_copy(d, s))
            FV()
        else:
            hs = s1b[:, 1:2] if which == 1 else s1b[:, 4:5]
            nh = s1b[:, 2:3] if which == 1 else s1b[:, 5:6]
            V(lambda e, s=src, n=n, nh=nh: e.tensor_single_scalar(Fs1[:, 0:n], s, nh, ALU.is_le))
            V(lambda e, d=dst, s=src, n=n, hs=hs: e.scalar_tensor_tensor(
                d, s, hs, Fs1[:, 0:n], ALU.is_ge, ALU.subtract))
        FV()
        rv = sV.n
        wgrp_free[g] = rv
        return rv

    def chunk_proj(dram, col0, width, nkc, wb, which, nm, lhs_of, bank_half_of):
        grp_rv = {0: wgroup(dram, col0, width, 0, wb, which)}
        for kc in range(nkc):
            if kc % 4 == 0:
                if kc + 4 < nkc:
                    grp_rv[kc + 4] = wgroup(dram, col0, width, kc + 4, wb, which)
                b.w("tensor", sV, grp_rv[kc])
            for mi in range(nm):
                bank, half = bank_half_of(mi)
                TE(lambda e, kc=kc, mi=mi, bank=bank, half=half: e.matmul(
                    PS[bank][:, half * width:(half + 1) * width],
                    lhs_of(kc, mi), wb[:, kc % 8, 0:width],
                    start=(kc == 0 and half == 0), stop=(kc == nkc - 1 and half == 1)),
                    inc=(mi == nm - 1))
            wbuf_free[wb_key(wb, kc % 8)] = sT.n
        return sT.n

    b.w("tensor", sV, xfeat_done)

    def rope_block(src3, cosm, sgm, n1, fcol2, dst_bf):
        b.w("vector", sT)
        V(lambda e: e.tensor_tensor(src3, src3, _bc_last(fcol2, 128), ALU.mult))
        V(lambda e: e.tensor_tensor(src3, src3, _bc_mid(n1, 2), ALU.mult))
        t1 = scr1[:, 0:256].rearrange("p (h d) -> p h d", h=2)
        u = scr1[:, 256:384].rearrange("p (h d) -> p h d", h=2)
        u2 = scr1[:, 384:512].rearrange("p (h d) -> p h d", h=2)
        V(lambda e: e.tensor_tensor(t1, src3, _bc_mid(cosm, 2), ALU.mult))
        V(lambda e: e.tensor_tensor(u, src3[:, :, 64:128], _bc_mid(sgm[:, 0:64], 2), ALU.mult))
        V(lambda e: e.tensor_tensor(u2, src3[:, :, 0:64], _bc_mid(sgm[:, 64:128], 2), ALU.mult))
        V(lambda e: e.tensor_tensor(t1[:, :, 0:64], t1[:, :, 0:64], u, ALU.add))
        V(lambda e: e.tensor_tensor(t1[:, :, 64:128], t1[:, :, 64:128], u2, ALU.add))
        V(lambda e: e.tensor_copy(dst_bf, scr1[:, 0:256]))
        return FV()

    def factors(nmh, with_rsq):
        VF()
        V(lambda e, n=nmh: e.tensor_scalar(fq[:, 0:n], fq[:, 0:n], 1.0 / 128.0, 1e-6, ALU.mult, ALU.add))
        FV()
        b.w("scalar", sV)
        SC(lambda e, n=nmh: e.activation(fq[:, 0:n], fq[:, 0:n], AF.Ln))
        FA()
        b.w("vector", sA)
        V(lambda e, n=nmh: e.tensor_scalar_mul(fq[:, 0:n], fq[:, 0:n], -0.5))
        FV()
        b.w("scalar", sV)
        SC(lambda e, n=nmh: e.activation(fq[:, 0:n], fq[:, 0:n], AF.Exp))
        FA()
        b.w("vector", sA)
        if with_rsq:
            V(lambda e, n=nmh: e.tensor_scalar_mul(fq[:, 0:n], fq[:, 0:n], RSQ_DH))
            VF()

    def prep_store(dst3):
        for hh in range(2):
            TE(lambda e, hh=hh: e.transpose(PB[:, hh * 128:(hh + 1) * 128],
                                            C_qbf[:, hh * 128:(hh + 1) * 128], ident),
               inc=(hh == 1))
        b.w("scalar", sT)
        SC(lambda e, d=dst3: e.copy(d, PB[:, 0:256].rearrange("p (h t) -> p h t", h=2)))
        ev = FA()
        b.w("tensor", sA, ev)

    for ns in range(2):
        b.w("tensor", sA)
        chunk_proj(wvT, ns * 256, 256, 16, wbP, None, NKT,
                   lambda kc, mk: xfeat[:, kc, mk * 128:(mk + 1) * 128],
                   lambda mk: (mk % 6, 0 if mk < 6 else 1))
        b.w("scalar", sT)
        for mk in range(NKT):
            bank, half = mk % 6, 0 if mk < 6 else 1
            SC(lambda e, mk=mk, ns=ns, bank=bank, half=half: e.copy(
                vtok[:, mk, ns * 256:(ns + 1) * 256], PS[bank][:, half * 256:(half + 1) * 256]))
        FA()

    for ns in range(2):
        b.w("tensor", sA)
        chunk_proj(wkT, ns * 256, 256, 16, wbP, None, NKT,
                   lambda kc, mk: xfeat[:, kc, mk * 128:(mk + 1) * 128],
                   lambda mk: (mk % 6, 0 if mk < 6 else 1))
        b.w("scalar", sT)
        b.w("scalar", sV)
        for mk in range(NKT):
            bank, half = mk % 6, 0 if mk < 6 else 1
            SC(lambda e, mk=mk, bank=bank, half=half: e.copy(
                kns[:, mk, :], PS[bank][:, half * 256:(half + 1) * 256]))
        FA()
        b.w("vector", sA)
        for mk in range(NKT):
            V(lambda e, mk=mk: e.tensor_mul(scr1[:, 0:256], kns[:, mk, :], kns[:, mk, :]))
            for hh in range(2):
                V(lambda e, mk=mk, hh=hh: e.tensor_reduce(
                    fq[:, mk * 2 + hh:mk * 2 + hh + 1], scr1[:, hh * 128:(hh + 1) * 128], AX.X, ALU.add))
        factors(NKT * 2, False)
        for mk in range(NKT):
            src3 = kns[:, mk, :].rearrange("p (h d) -> p h d", h=2)
            cv = rope_block(src3, ck[:, mk, :], sgk[:, mk, :], kn1, fq[:, mk * 2:mk * 2 + 2], C_qbf[:, 0:256])
            b.w("tensor", sV, cv)
            prep_store(kf[:, ns * 2:ns * 2 + 2, mk * 128:(mk + 1) * 128])
            if DBG == "qbf8" and ns == 0 and mk == 8:
                for s_ in (sT, sV, sA):
                    b.w("gpsimd", s_)
                b.dma("gpsimd", dbgf[:, 0:256], C_qbf[:, 0:256], sS)
                b.dma("gpsimd", dbgf[:, 256:512], scr1[:, 0:256], sS)
                b.dma("gpsimd", dbgf[:, 512:768], kns[:, 8, :], sS)
                b.dma("gpsimd", dbgf[:, 768:770], fq[:, 16:18], sS)
                b.dma("gpsimd", dbgb[:, 0:128], kf[:, 0, 8 * 128:9 * 128], sS)
                b.dma("gpsimd", dbgb[:, 128:256], kf[:, 1, 8 * 128:9 * 128], sS)
                for e_ in ("tensor", "vector", "scalar", "sync"):
                    b.w(e_, sS)

    tap("kns", Frf[:, 0:3072], 3072)
    tap("fqk", Fc[:, 0:512], 512)
    for ns in range(8):
        b.w("tensor", sA)
        chunk_proj(wqT, ns * 256, 256, 16, wbP, None, NT,
                   lambda kc, m: xfeat[:, kc, (4 + m) * 128:(5 + m) * 128],
                   lambda m: (m // 2, m % 2))
        b.w("scalar", sT)
        b.w("scalar", sV)
        for m in range(NT):
            bank, half = m // 2, m % 2
            SC(lambda e, m=m, bank=bank, half=half: e.copy(
                scr0[:, m * 256:(m + 1) * 256], PS[bank][:, half * 256:(half + 1) * 256]))
        FA()
        b.w("vector", sA)
        for m in range(NT):
            V(lambda e, m=m: e.tensor_mul(scr1[:, 0:256], scr0[:, m * 256:(m + 1) * 256],
                                          scr0[:, m * 256:(m + 1) * 256]))
            for hh in range(2):
                V(lambda e, m=m, hh=hh: e.tensor_reduce(
                    fq[:, m * 2 + hh:m * 2 + hh + 1], scr1[:, hh * 128:(hh + 1) * 128], AX.X, ALU.add))
        factors(NT * 2, True)
        for m in range(NT):
            src3 = scr0[:, m * 256:(m + 1) * 256].rearrange("p (h d) -> p h d", h=2)
            cv = rope_block(src3, cq[:, m, :], sgq[:, m, :], qn1, fq[:, m * 2:m * 2 + 2], C_qbf[:, 0:256])
            b.w("tensor", sV, cv)
            prep_store(qf[:, ns * 2:ns * 2 + 2, m * 128:(m + 1) * 128])

    tap("qf", R1[:, 0:16384], 16384)
    tap("kf", R1[:, 16384:22528], 6144)
    tap("vtok", R1[:, 22528:28672], 6144)
    b.w("tensor", sA)
    b.w("tensor", sV)
    pv_free = 0
    sc_free = [(None, 0), (None, 0)]
    ctx_free = (None, 0)
    l_free = (None, 0)
    lb_free = (None, 0)

    for v_ in range(2):
        for g in range(HKV):
            for hh in range(4):
                h = g * 4 + hh
                exp_last = 0
                for r in range(NREL):
                    kt = 4 * v_ + r
                    bk = r % 2
                    fv = sc_free[bk]
                    if fv[0] is not None:
                        b.w("tensor", fv[0], fv[1])
                    mmv = TE(lambda e, g=g, h=h, kt=kt, bk=bk, v_=v_: e.matmul(
                        PS[bk][:, 0:512], kf[:, g, kt * 128:(kt + 1) * 128],
                        qf[:, h, v_ * 512:(v_ + 1) * 512], start=True, stop=True))
                    b.w("scalar", sT, mmv)
                    if pv_free:
                        b.w("scalar", sT, pv_free)
                    ev = SC(lambda e, r=r, bk=bk: e.activation(pslot[:, r, :], PS[bk][:, 0:512], AF.Exp))
                    sc_free[bk] = (sA, ev)
                    exp_last = ev
                FA()
                b.w("vector", sA)
                for r in range(NREL):
                    for s_ in range(4):
                        cc = s_ + 4 - r
                        if cc in (1, 2, 3):
                            continue
                        if cc == 0:
                            V(lambda e, r=r, s_=s_: e.tensor_mul(
                                pslot[:, r, s_ * 128:(s_ + 1) * 128],
                                pslot[:, r, s_ * 128:(s_ + 1) * 128], tri0))
                        elif cc == 4:
                            V(lambda e, r=r, s_=s_: e.tensor_mul(
                                pslot[:, r, s_ * 128:(s_ + 1) * 128],
                                pslot[:, r, s_ * 128:(s_ + 1) * 128], tri4))
                        else:
                            V(lambda e, r=r, s_=s_: e.memset(
                                pslot[:, r, s_ * 128:(s_ + 1) * 128], 0.0))
                FV()
                mask_done = sV.n
                b.w("tensor", sV, mask_done)
                if l_free[0] is not None:
                    b.w("tensor", l_free[0], l_free[1])
                if ctx_free[0] is not None:
                    b.w("tensor", ctx_free[0], ctx_free[1])
                for r in range(NREL):
                    kt = 4 * v_ + r
                    TE(lambda e, r=r: e.matmul(PS[3][0:1, 0:512], onesb, pslot[:, r, :],
                                               start=(r == 0), stop=(r == NREL - 1)),
                       inc=(r == NREL - 1))
                    TE(lambda e, r=r, kt=kt, g=g: e.matmul(
                        PS[2][:, 0:512], vtok[:, kt, g * 128:(g + 1) * 128], pslot[:, r, :],
                        start=(r == 0), stop=(r == NREL - 1)), inc=(r == NREL - 1))
                pv_free = sT.n
                lv = sT.n - 1
                b.w("vector", sT, lv)
                V(lambda e, v_=v_: e.tensor_tensor(
                    lrow[0:1, :], PS[3][0:1, 0:512], Fpc[0:1, v_ * 512:(v_ + 1) * 512], ALU.subtract))
                lev = FV()
                l_free = (sV, lev)
                b.w("tensor", sV, lev)
                if lb_free[0] is not None:
                    b.w("tensor", lb_free[0], lb_free[1])
                bcv = TE(lambda e: e.matmul(PS[4][:, 0:512], onesr[0:1, :], lrow[0:1, :],
                                            start=True, stop=True))
                b.w("vector", sT, bcv)
                rv = V(lambda e: e.reciprocal(lsb, PS[4][:, 0:512]))
                lb_free = (sV, rv)
                V(lambda e, h=h, v_=v_: e.tensor_tensor(
                    ctxf[:, h, v_ * 512:(v_ + 1) * 512], PS[2][:, 0:512], lsb, ALU.mult))
                cn = FV()
                ctx_free = (sV, cn)

    tap("ctxf", R2[:, 0:16384], 16384)
    b.w("tensor", sV)
    for ns in range(8):
        b.w("tensor", sA)
        chunk_proj(woT, ns * 256, 256, 16, wbP, None, NT,
                   lambda kc, m: ctxf[:, kc, m * 128:(m + 1) * 128],
                   lambda m: (m // 2, m % 2))
        b.w("scalar", sT)
        for m in range(NT):
            bank, half = m // 2, m % 2
            SC(lambda e, m=m, ns=ns, bank=bank, half=half: e.copy(
                aout[:, m, ns * 256:(ns + 1) * 256], PS[bank][:, half * 256:(half + 1) * 256]))
        FA()

    tap("aout", R1[:, 0:16384], 16384)
    oproj_A = sA.n
    b.w("gpsimd", sV)
    xr_used = [0, 0]
    for m in range(NT):
        sl = m % 2
        if m >= 2:
            b.w("gpsimd", sV, xr_used[sl])
        ld = b.dma("gpsimd", xrb[:, sl, :], xr[m * 128:(m + 1) * 128, :], sG)
        b.w("vector", sG, ld)
        if m == 0:
            b.w("vector", sA, oproj_A)
        V(lambda e, m=m, sl=sl: e.tensor_tensor(scr0, aout[:, m, :], xrb[:, sl, :], ALU.add))
        xr_used[sl] = sV.n
        V(lambda e: e.tensor_reduce(tin[:, 8:9], scr0, AX.X, ALU.add))
        VF()
        V(lambda e: e.tensor_scalar_mul(tin[:, 8:9], tin[:, 8:9], -1.0 / 2048.0))
        VF()
        V(lambda e: e.tensor_single_scalar(scr0, scr0, tin[:, 8:9], ALU.add))
        V(lambda e: e.tensor_mul(scr1[:, 0:1024], scr0[:, 0:1024], scr0[:, 0:1024]))
        V(lambda e: e.tensor_reduce(tin[:, 9:10], scr1[:, 0:1024], AX.X, ALU.add))
        V(lambda e: e.tensor_mul(scr1[:, 0:1024], scr0[:, 1024:2048], scr0[:, 1024:2048]))
        V(lambda e: e.tensor_reduce(tin[:, 10:11], scr1[:, 0:1024], AX.X, ALU.add))
        VF()
        V(lambda e: e.tensor_add(tin[:, 9:10], tin[:, 9:10], tin[:, 10:11]))
        VF()
        V(lambda e: e.tensor_scalar(tin[:, 9:10], tin[:, 9:10], 1.0 / 2048.0, 1e-5, ALU.mult, ALU.add))
        FV()
        if DBG == "ln7" and m == 7:
            for s_ in (sT, sV, sA):
                b.w("gpsimd", s_)
            b.dma("gpsimd", dbgf[:, 0:2048], Fs0[:, :], sS)
            b.dma("gpsimd", dbgf[:, 2048:3072], Fs1[:, :], sS)
            b.dma("gpsimd", dbgf[:, 3072:3112], tin, sS)
            for e_ in ("tensor", "vector", "scalar", "sync"):
                b.w(e_, sS)
        b.w("scalar", sV)
        SC(lambda e: e.activation(tin[:, 9:10], tin[:, 9:10], AF.Ln))
        FA()
        b.w("vector", sA)
        V(lambda e: e.tensor_scalar_mul(tin[:, 9:10], tin[:, 9:10], -0.5))
        FV()
        b.w("scalar", sV)
        SC(lambda e: e.activation(tin[:, 9:10], tin[:, 9:10], AF.Exp))
        FA()
        b.w("vector", sA)
        V(lambda e, m=m: e.tensor_single_scalar(r1b[:, m, :], scr0, tin[:, 9:10], ALU.mult))
        FV()

    tap("r1b", R4[:, 0:16384], 16384)
    if DBG == "r1b":
        b.dma("gpsimd", dbgf[:, 0:4096], Ftb[:, 0:4096], sS)
        b.dma("gpsimd", dbgf[:, 4096:6144], Fs0[:, :], sS)
        b.dma("gpsimd", dbgf[:, 6144:6184], tin, sS)
        for e_ in ("tensor", "vector", "scalar", "sync"):
            b.w(e_, sS)
    tpz_free = [0]

    def transpose_pack(src_bf, ntiles, dst_of_pack):
        b.w("tensor", sV)
        for t0 in range(0, ntiles, 8):
            npk = min(8, ntiles - t0)
            if tpz_free[0]:
                b.w("tensor", sA, tpz_free[0])
            for j in range(npk):
                TE(lambda e, t=t0 + j, j=j: e.transpose(
                    PB[:, j * 128:(j + 1) * 128], src_bf[:, t * 128:(t + 1) * 128], ident),
                   inc=(j == npk - 1))
            b.w("scalar", sT)
            dst_of_pack(t0 // 8, npk)
            FA()
            tpz_free[0] = sA.n

    def round_quant(zsrc, nblocks, sig_col, xq_write):
        for blk in range(nblocks):
            zb = zsrc[:, blk * 1024:(blk + 1) * 1024]
            b.w("vector", sT)
            V(lambda e, zb=zb, sc=sig_col: e.tensor_single_scalar(Gi[:, 0:1024], zb, sc, ALU.mult))
            V(lambda e: e.tensor_copy(gf, Gi[:, 0:1024]))
            V(lambda e, zb=zb, sc=sig_col: e.scalar_tensor_tensor(dd, zb, sc, gf, ALU.mult, ALU.subtract))
            V(lambda e: e.scalar_tensor_tensor(gf, dd, 0.5, gf, ALU.is_ge, ALU.add))
            V(lambda e: e.tensor_scalar(dd, dd, -0.5, -1.0, ALU.is_le, ALU.mult))
            V(lambda e: e.tensor_add(gf, gf, dd))
            V(lambda e: e.tensor_scalar(xqt, gf, 127.0, -127.0, ALU.min, ALU.max))
            FV()
            xq_write(blk)

    def butterflies(curA, curB, nstages, width):
        cur, nxt = curA, curB
        for st in range(nstages):
            half = (1 << st) * 128
            for gpi in range(width // (2 * half)):
                base = gpi * 2 * half
                V(lambda e, c=cur, n=nxt, base=base, half=half: e.tensor_tensor(
                    n[:, base:base + half], c[:, base:base + half],
                    c[:, base + half:base + 2 * half], ALU.add))
                V(lambda e, c=cur, n=nxt, base=base, half=half: e.tensor_tensor(
                    n[:, base + half:base + 2 * half], c[:, base:base + half],
                    c[:, base + half:base + 2 * half], ALU.subtract))
            cur, nxt = nxt, cur
        return cur

    def stageA(src3, nkc, zdst):
        for kc in range(nkc):
            bk = (kc // 4) % 2
            if kc % 4 == 0:
                fv = sc_free[bk]
                if fv[0] is not None:
                    b.w("tensor", fv[0], fv[1])
            TE(lambda e, kc=kc, bk=bk: e.matmul(
                PS[bk][:, (kc % 4) * 128:(kc % 4 + 1) * 128], src3[:, kc, :], h128,
                start=True, stop=True), inc=(kc % 4 == 3))
            if kc % 4 == 3:
                b.w("scalar", sT)
                SC(lambda e, kc=kc, bk=bk, z=zdst: e.copy(
                    z[:, (kc - 3) * 128:(kc + 1) * 128], PS[bk][:, 0:512]))
                ev = FA()
                sc_free[bk] = (sA, ev)

    for m in range(NT):
        def _dst_r1(pk, npk, m=m):
            SC(lambda e, pk=pk, npk=npk: e.copy(
                r1f.rearrange("p k t -> p (k t)")[:, pk * 1024:pk * 1024 + npk * 128],
                PB[:, 0:npk * 128]))
        transpose_pack(r1b[:, m, :], 16, _dst_r1)
        b.w("tensor", sA)
        stageA(r1f, 16, zA1)
        b.w("vector", sA)
        zf = butterflies(zA1, zB1, 4, 2048)
        V(lambda e, z=zf: e.tensor_reduce(tin[:, 12:13], z[:, 0:2048], AX.X, ALU.max,
                                          apply_absolute_value=True))
        VF()
        V(lambda e: e.tensor_scalar(tin[:, 13:14], tin[:, 12:13], ISQ1, 1e-5, ALU.mult, ALU.max))
        VF()
        V(lambda e: e.reciprocal(tin[:, 14:15], tin[:, 13:14]))
        VF()
        V(lambda e: e.tensor_scalar_mul(tin[:, 15:16], tin[:, 14:15], 127.0 * ISQ1))
        V(lambda e: e.tensor_scalar_mul(tin[:, 16:17], tin[:, 13:14], 1.0 / 127.0))
        VF()
        V(lambda e: e.tensor_mul(tin[:, 17:18], tin[:, 16:17], s1b[:, 0:1]))
        VF()
        V(lambda e, m=m: e.tensor_mul(A2c[:, m:m + 1], tin[:, 17:18], tin[:, 17:18]))
        VF()

        def _xq_write1(blk, m=m):
            def _dst(pk, npk, m=m, blk=blk):
                SC(lambda e, pk=pk, npk=npk, m=m, blk=blk: e.copy(
                    xq1f[:, blk * 8:blk * 8 + npk, m * 128:(m + 1) * 128],
                    PB[:, 0:npk * 128].rearrange("p (k t) -> p k t", k=npk)))
            transpose_pack(xqt, 8, _dst)
        round_quant(zf, 2, tin[:, 15:16], _xq_write1)

    tap("xq1f", R1[:, 0:16384], 16384)
    b.w("tensor", sA)
    for ns in range(16):
        b.w("tensor", sV)
        mm = chunk_proj(wuT, ns * 256, 256, 16, wbF, 1, NT,
                        lambda kc, m: xq1f[:, kc, m * 128:(m + 1) * 128],
                        lambda m: (m // 2, m % 2))
        b.w("vector", sT, mm)
        for m in range(NT):
            bank, half = m // 2, m % 2
            V(lambda e, bank=bank, half=half: e.tensor_scalar_max(
                scr1[:, 0:256], PS[bank][:, half * 256:(half + 1) * 256], 0.0))
            V(lambda e, m=m, ns=ns: e.tensor_mul(
                X2(m)[:, ns * 256:(ns + 1) * 256], scr1[:, 0:256], scr1[:, 0:256]))
        FV()

    if DBG == "h2":
        for mm_ in range(NT):
            tap("h2", X2(mm_), 4096) if False else None
        for s_ in (sT, sV, sA):
            b.w("gpsimd", s_)
        for mm_ in range(NT):
            b.dma("gpsimd", dbgb[:, mm_ * 4096:(mm_ + 1) * 4096], X2(mm_), sS)
        for e_ in ("tensor", "vector", "scalar", "sync"):
            b.w(e_, sS)
    FV()
    b.w("tensor", sV)
    for m in range(NT):
        def _dst_h2(pk, npk, m=m):
            SC(lambda e, pk=pk, npk=npk: e.copy(
                h2f.rearrange("p k t -> p (k t)")[:, pk * 1024:pk * 1024 + npk * 128],
                PB[:, 0:npk * 128]))
        transpose_pack(X2(m), 32, _dst_h2)
        b.w("tensor", sA)
        stageA(h2f, 32, zA2)
        b.w("vector", sA)
        zf = butterflies(zA2, zB2, 5, 4096)
        V(lambda e, z=zf: e.tensor_reduce(tin[:, 12:13], z[:, 0:4096], AX.X, ALU.max,
                                          apply_absolute_value=True))
        VF()
        V(lambda e, m=m: e.tensor_mul(tin[:, 13:14], tin[:, 12:13], A2c[:, m:m + 1]))
        VF()
        V(lambda e: e.tensor_scalar(tin[:, 13:14], tin[:, 13:14], ISQ2, 1e-5, ALU.mult, ALU.max))
        VF()
        V(lambda e: e.reciprocal(tin[:, 14:15], tin[:, 13:14]))
        VF()
        V(lambda e, m=m: e.tensor_mul(tin[:, 15:16], tin[:, 14:15], A2c[:, m:m + 1]))
        VF()
        V(lambda e: e.tensor_scalar_mul(tin[:, 15:16], tin[:, 15:16], 127.0 * ISQ2))
        V(lambda e: e.tensor_scalar_mul(tin[:, 16:17], tin[:, 13:14], 1.0 / 127.0))
        VF()
        V(lambda e, m=m: e.tensor_mul(FAc[:, m:m + 1], tin[:, 16:17], s1b[:, 3:4]))
        VF()

        def _xq_write2(blk, m=m):
            def _dst(pk, npk, m=m, blk=blk):
                SC(lambda e, pk=pk, npk=npk, m=m, blk=blk: e.copy(
                    X2(m)[:, blk * 1024 + pk * 1024:blk * 1024 + pk * 1024 + npk * 128],
                    PB[:, 0:npk * 128]))
            transpose_pack(xqt, 8, _dst)
        round_quant(zf, 4, tin[:, 15:16], _xq_write2)

    if DBG == "xq2":
        for s_ in (sT, sV, sA):
            b.w("gpsimd", s_)
        for mm_ in range(NT):
            b.dma("gpsimd", dbgb[:, mm_ * 4096:(mm_ + 1) * 4096], X2(mm_), sS)
        for e_ in ("tensor", "vector", "scalar", "sync"):
            b.w(e_, sS)
    b.w("tensor", sA)
    b.w("vector", sA)
    V(lambda e: e.memset(mnc[:, 0:8], 0.0))
    dwn_free = [0] * 8
    for nq in range(8):
        b.w("tensor", sV)
        mm = chunk_proj(wdT, nq * 256, 256, 32, wbF, 2, NT,
                        lambda kc, m: X2(m).rearrange("p (k t) -> p k t", k=32)[:, kc, :],
                        lambda m: (m // 2, m % 2))
        b.w("vector", sT, mm)
        for m in range(NT):
            bank, half = m // 2, m % 2
            if dwn_free[m]:
                b.w("vector", sS, dwn_free[m])
            V(lambda e, m=m, bank=bank, half=half, nq=nq: e.scalar_tensor_tensor(
                dwn[:, m, :], PS[bank][:, half * 256:(half + 1) * 256], FAc[:, m:m + 1],
                r1b[:, m, nq * 256:(nq + 1) * 256], ALU.mult, ALU.add))
            V(lambda e, m=m: e.tensor_reduce(tin[:, 18:19], dwn[:, m, :], AX.X, ALU.add))
            VF()
            V(lambda e, m=m: e.tensor_add(mnc[:, m:m + 1], mnc[:, m:m + 1], tin[:, 18:19]))
            vv = FV()
            b.w("gpsimd", sV, vv)
            dwn_free[m] = b.dma("gpsimd", y[m * 128:(m + 1) * 128, nq * 256:(nq + 1) * 256],
                                dwn[:, m, :], sS)

    down_stores = sS.n
    ybuf = Ftb[:, 0:2048]
    ln2_used = 0
    for m in range(NT):
        if m == 0:
            b.w("gpsimd", sS, down_stores)
        if ln2_used:
            b.w("gpsimd", sV, ln2_used)
        ld = b.dma("gpsimd", ybuf, y[m * 128:(m + 1) * 128, :], sG)
        b.w("vector", sG, ld)
        VF()
        V(lambda e, m=m: e.tensor_scalar_mul(tin[:, 19:20], mnc[:, m:m + 1], -1.0 / 2048.0))
        VF()
        V(lambda e, m=m: e.tensor_single_scalar(scr0, ybuf, tin[:, 19:20], ALU.add))
        ln2_used = sV.n
        V(lambda e: e.tensor_mul(scr1[:, 0:1024], scr0[:, 0:1024], scr0[:, 0:1024]))
        V(lambda e: e.tensor_reduce(tin[:, 20:21], scr1[:, 0:1024], AX.X, ALU.add))
        V(lambda e: e.tensor_mul(scr1[:, 0:1024], scr0[:, 1024:2048], scr0[:, 1024:2048]))
        V(lambda e: e.tensor_reduce(tin[:, 21:22], scr1[:, 0:1024], AX.X, ALU.add))
        VF()
        V(lambda e: e.tensor_add(tin[:, 20:21], tin[:, 20:21], tin[:, 21:22]))
        VF()
        V(lambda e: e.tensor_scalar(tin[:, 20:21], tin[:, 20:21], 1.0 / 2048.0, 1e-5, ALU.mult, ALU.add))
        FV()
        b.w("scalar", sV)
        SC(lambda e: e.activation(tin[:, 20:21], tin[:, 20:21], AF.Ln))
        FA()
        b.w("vector", sA)
        V(lambda e: e.tensor_scalar_mul(tin[:, 20:21], tin[:, 20:21], -0.5))
        FV()
        b.w("scalar", sV)
        SC(lambda e: e.activation(tin[:, 20:21], tin[:, 20:21], AF.Exp))
        FA()
        b.w("vector", sA)
        V(lambda e: e.tensor_single_scalar(outf, scr0, tin[:, 20:21], ALU.mult))
        ov = FV()
        b.w("gpsimd", sV, ov)
        st = b.dma("gpsimd", y[m * 128:(m + 1) * 128, :], outf, sS)
        b.w("vector", sS, st)

    b.w("gpsimd", sS, sS.n)

    with nc.Block() as block:
        for dec, e in ((block.sync, "sync"), (block.scalar, "scalar"), (block.vector, "vector"),
                       (block.tensor, "tensor"), (block.gpsimd, "gpsimd")):
            def _body(eng, cmds=b.cmds[e]):
                for f in cmds:
                    f(eng)
            dec(_body)
    for cm in reversed(ctx):
        cm.__exit__(None, None, None)
    for cm in reversed(semc):
        cm.__exit__(None, None, None)
    return nc


def _host_tables():
    import jax
    import jax.numpy as jnp
    with jax.default_device(jax.devices("cpu")[0]):
        inv_freq = 1.0 / (10000.0 ** (jnp.arange(0, DH, 2, dtype=jnp.float32) / DH))
        pos = jnp.arange(T, dtype=jnp.float32)
        freqs = pos[:, None] * inv_freq[None, :]
        emb = jnp.concatenate([freqs, freqs], axis=-1)
        cos = np.asarray(jnp.cos(emb), dtype=np.float32)
        sin = np.asarray(jnp.sin(emb), dtype=np.float32)
    sg = sin.copy()
    sg[:, :64] = -sg[:, :64]
    return cos, sg


def _hadamard128():
    hh = np.array([[1.0]], dtype=np.float32)
    while hh.shape[0] < 128:
        hh = np.block([[hh, hh], [hh, -hh]])
    return hh.astype(ml_dtypes.bfloat16)


_NC_CACHE = [None]


def kernel(L0_attn_x, L0_attn_Wq, L0_attn_Wk, L0_attn_Wv, L0_attn_Wo,
           L0_attn_qn_w, L0_attn_kn_w, L0_ffn_w_up, L0_ffn_w_down):
    x = np.ascontiguousarray(np.asarray(L0_attn_x, dtype=np.float32)[0])
    xTf = np.ascontiguousarray(x.T)
    cos, sg = _host_tables()
    tk = np.arange(128)[:, None]
    tq = np.arange(128)[None, :]
    tri = np.concatenate([(tk <= tq).astype(np.float32), (tk >= tq).astype(np.float32)],
                         axis=1).astype(ml_dtypes.bfloat16)
    common = {
        "wqT": np.ascontiguousarray(np.asarray(L0_attn_Wq, np.float32).T),
        "wkT": np.ascontiguousarray(np.asarray(L0_attn_Wk, np.float32).T),
        "wvT": np.ascontiguousarray(np.asarray(L0_attn_Wv, np.float32).T),
        "woT": np.ascontiguousarray(np.asarray(L0_attn_Wo, np.float32).T),
        "wuT": np.ascontiguousarray(np.asarray(L0_ffn_w_up, np.float32).T),
        "wdT": np.ascontiguousarray(np.asarray(L0_ffn_w_down, np.float32).T),
        "qnr": np.ascontiguousarray(np.broadcast_to(
            np.asarray(L0_attn_qn_w, np.float32)[None, :], (128, DH))),
        "knr": np.ascontiguousarray(np.broadcast_to(
            np.asarray(L0_attn_kn_w, np.float32)[None, :], (128, DH))),
        "h128": _hadamard128(),
        "ident": np.eye(128, dtype=ml_dtypes.bfloat16),
        "onesf": np.ones((128, 2), np.float32),
        "onesr": np.ones((1, 128), np.float32),
        "onesb": np.ones((128, 1), ml_dtypes.bfloat16),
        "tri": tri,
    }
    in_maps = []
    for c in range(NCORES):
        q0 = c * TOK
        kv0 = q0 - HALO
        if kv0 >= 0:
            xTs = np.ascontiguousarray(xTf[:, kv0:q0 + TOK])
            cosk = np.ascontiguousarray(cos[kv0:q0 + TOK])
            sgk = np.ascontiguousarray(sg[kv0:q0 + TOK])
        else:
            xTs = np.concatenate([np.zeros((D, HALO), np.float32), xTf[:, 0:TOK]], axis=1)
            cosk = np.concatenate([np.zeros((HALO, DH), np.float32), cos[0:TOK]], axis=0)
            sgk = np.concatenate([np.zeros((HALO, DH), np.float32), sg[0:TOK]], axis=0)
        padc = np.zeros((1, 1024), np.float32)
        if c == 0:
            qg = np.arange(1024)
            padc[0, :] = np.maximum(0, HALO - qg)
        m = dict(common)
        m.update({
            "xT": xTs, "xr": np.ascontiguousarray(x[q0:q0 + TOK]),
            "cosq": np.ascontiguousarray(cos[q0:q0 + TOK]),
            "sgq": np.ascontiguousarray(sg[q0:q0 + TOK]),
            "cosk": cosk, "sgk": sgk,
            "padc": padc.astype(np.float16),
        })
        in_maps.append(m)

    if _NC_CACHE[0] is None:
        _NC_CACHE[0] = build_nc()
    res = run_bass_kernel_spmd(_NC_CACHE[0], in_maps, core_ids=list(range(NCORES)))
    out = np.concatenate([np.asarray(r["y"], np.float32) for r in res.results], axis=0)
    return out.reshape(1, T, D)
